# version 12
# baseline (speedup 1.0000x reference)
import math
import contextlib
import numpy as np
import concourse.bass as bass
import concourse.mybir as mybir
from concourse.bass_utils import run_bass_kernel_spmd

F32 = mybir.dt.float32
BF16 = mybir.dt.bfloat16
FP8 = mybir.dt.float8e4
AF = mybir.ActivationFunctionType
ALU = mybir.AluOpType
AX = mybir.AxisListType

SEQ = 2048
DM = 2048
NHEAD = 16
HD = 128
FF = 8192
NT = SEQ // 128
ND = DM // 128
NQ = SEQ // 512
NF = FF // 128
ISQ = 1.0 / math.sqrt(HD)
ESHIFT = -12.0
EPS = 1e-5
NW1 = DM * FF

ENGS = ("T", "V", "S", "Y", "G")
DMA_SEMS = ("dy", "dg", "do")


class Plan:
    def __init__(self):
        self.fns = {e: [] for e in ENGS}
        self.cnt = {s: 0 for s in ("t", "v", "s") + DMA_SEMS}
        self.lw = {e: {} for e in ENGS}
        self.sem = {}

    def raw(self, eng, fn):
        self.fns[eng].append(fn)

    def wait(self, eng, tok):
        if tok is None:
            return
        s, v = tok
        if v <= 0:
            return
        assert v <= self.cnt[s], f"wait-future {eng} {s} {v}>{self.cnt[s]}"
        if self.lw[eng].get(s, 0) >= v:
            return
        self.lw[eng][s] = v
        h = self.sem[s]
        self.raw(eng, lambda E, _h=h, _v=v: E.wait_ge(_h, _v))

    def op(self, eng, fn, sem=None, k=1):
        if sem is None:
            self.raw(eng, fn)
            return None
        self.cnt[sem] += k
        v = self.cnt[sem]
        h = self.sem[sem]
        self.raw(eng, lambda E, _f=fn, _h=h, _k=k: _f(E).then_inc(_h, _k))
        return (sem, v)

    def tok(self, sem):
        return (sem, self.cnt[sem])


def build(taps=()):
    nc = bass.Bass("TRN2", target_bir_lowering=False)

    x_d = nc.dram_tensor("x", [SEQ, DM], F32, kind="ExternalInput").ap()
    wqkvh_d = nc.dram_tensor("wqkvh", [16, 128, 6144], F32, kind="ExternalInput").ap()
    wo_d = nc.dram_tensor("wo", [DM, DM], F32, kind="ExternalInput").ap()
    w1qin_d = nc.dram_tensor("w1qin", [NF, 128, 2048], F32, kind="ExternalInput").ap()
    w2t_d = nc.dram_tensor("w2t", [FF, DM], F32, kind="ExternalInput").ap()
    bq_d = nc.dram_tensor("bq", [48, 128], F32, kind="ExternalInput").ap()
    bv_d = nc.dram_tensor("bvrow", [1, DM], F32, kind="ExternalInput").ap()
    bo_d = nc.dram_tensor("borow", [1, DM], F32, kind="ExternalInput").ap()
    b1_d = nc.dram_tensor("b1", [64, 128], F32, kind="ExternalInput").ap()
    b2_d = nc.dram_tensor("b2row", [1, DM], F32, kind="ExternalInput").ap()
    mm_d = nc.dram_tensor("maskm", [128, 896], F32, kind="ExternalInput").ap()
    id_d = nc.dram_tensor("ident", [128, 128], F32, kind="ExternalInput").ap()
    out_d = nc.dram_tensor("out", [SEQ, DM], F32, kind="ExternalOutput").ap()
    w1q8 = nc.dram_tensor("w1q8", [NF, 128, DM], FP8, kind="Internal").ap()
    w2q8 = nc.dram_tensor("w2q8", [4, NF, 128, 512], FP8, kind="Internal").ap()
    tap_d = {}
    for tname, shape in dict(
        ln1=[128, 32768], lntm=[128, 2048], stats=[128, 64], qkv0=[128, 3 * 2048], o=[128, 32768],
        proj=[128, 32768], y=[128, 32768], g0=[128, 32768],
        thr=[128, 4], den0=[1, 512],
    ).items():
        if tname in taps:
            tap_d[tname] = nc.dram_tensor("tap_" + tname, shape, F32,
                                          kind="ExternalOutput").ap()

    es = contextlib.ExitStack()
    sb = lambda name, shape, dt: es.enter_context(nc.sbuf_tensor(name, shape, dt))
    psb = lambda name, shape, dt: es.enter_context(nc.psum_tensor(name, shape, dt))

    U1 = sb("U1", [128, 32768], BF16)
    U2 = sb("U2", [128, 32768], BF16)
    U3 = sb("U3", [128, 8192], BF16)
    U4 = sb("U4", [128, 4096], F32)
    U5 = sb("U5", [128, 8192], BF16)
    LNTM = sb("LNTM", [128, 2048], F32)
    CBUF = sb("CBUF", [128, 4608], BF16)
    KB = sb("KB", [128, 2048], BF16)
    QB = sb("QB", [128, 2048], BF16)
    VB = sb("VB", [128, 2048], BF16)
    IDENTF = sb("IDENTF", [128, 128], F32)
    IDENTB = sb("IDENTB", [128, 128], BF16)
    WSTATP = sb("WSTATP", [128, 128], F32)
    BQS = sb("BQS", [128, 48], F32)
    B1S = sb("B1S", [128, 64], F32)
    BNS = sb("BNS", [128, 48], F32)
    AGG = sb("AGG", [128, 4], F32)
    RSV = sb("RSV", [128, 2], F32)
    NMRS = sb("NMRS", [128, 2], F32)
    CONSTC = sb("CONSTC", [128, 2], F32)
    THR = sb("THR", [128, 4], F32)
    ONES1T = sb("ONES1T", [1, 128], BF16)
    ONESCOLB = sb("ONESCOLB", [128, 1], BF16)
    ONES1F = sb("ONES1F", [1, 128], F32)
    ONESCF2 = sb("ONESCF2", [128, 1], F32)
    TOT = LNTM[0:1, 600:602]

    SW0 = psb("sw0", [128, 1024], F32)
    SW1 = psb("sw1", [128, 1024], F32)
    SW2 = psb("sw2", [128, 1024], F32)
    PO = psb("po", [128, 512], F32)
    PDEN = psb("pden", [128, 512], F32)
    PSAP = {0: PO[:], 1: PDEN[:], 2: SW0[:, 0:512], 3: SW0[:, 512:1024],
            4: SW1[:, 0:512], 5: SW1[:, 512:1024],
            6: SW2[:, 0:512], 7: SW2[:, 512:1024]}

    p = Plan()
    for s in p.cnt:
        p.sem[s] = es.enter_context(nc.semaphore("sem_" + s))

    ENG_DMA = {"Y": "dy", "G": "dg"}

    def dma(eng, out, in_, waits=(), sem=None):
        for w in waits:
            p.wait(eng, w)
        sem = sem or ENG_DMA[eng]
        return p.op(eng, lambda E, _o=out, _i=in_: E.dma_start(out=_o, in_=_i), sem, 16)

    def dma_nc(eng, out, in_, waits=(), sem=None):
        for w in waits:
            p.wait(eng, w)
        sem = sem or ENG_DMA[eng]
        return p.op(eng, lambda E, _o=out, _i=in_: E.dma_start(
            out=_o, in_=_i, allow_slow_non_contiguous=True), sem, 16)

    bank = [{"w": None, "r": None} for _ in range(8)]
    PSA = lambda b: PSAP[b]
    ptab = {"A": {"w": [None] * 8, "r": [None] * 8},
            "B": {"w": [None] * 8, "r": [None] * 8}}

    tok_mask = dma("G", CBUF[:, 0:896], mm_d)
    tok_id = dma("Y", IDENTF[:], id_d)
    tok_idb = dma("G", IDENTB[:], id_d)
    tok_bq = dma_nc("Y", BQS[:], bq_d.rearrange("c p -> p c"))
    tok_b1 = dma_nc("Y", B1S[:], b1_d.rearrange("c p -> p c"))
    p.op("V", lambda E: E.memset(CONSTC[:, 0:1], EPS))
    p.op("V", lambda E: E.memset(CONSTC[:, 1:2], ESHIFT))
    p.op("V", lambda E: E.memset(ONES1T[:], 1.0))
    p.op("V", lambda E: E.memset(ONESCOLB[:], 1.0))
    p.op("V", lambda E: E.memset(ONESCF2[:], 1.0))
    tok_consts = p.op("V", lambda E: E.memset(ONES1F[:], 1.0), "v")
    ONES1 = ONES1T[0:1, :]

    u4tok = [None, None]
    lntm_free = None
    lnguard = [None, None]

    def ln_stats(eng_src_ap, slot, wait_toks, out=None):
        for w in wait_toks:
            p.wait("V", w)
        p.wait("V", lnguard[slot])
        last = None
        for sg in range(4):
            a, b = slot * 24 + 6 * sg, slot * 24 + 6 * sg + 6
            last = p.op("V", lambda E, _a=a, _b=b, _sg=sg, _ap=eng_src_ap: E.bn_stats(
                BNS[:, _a:_b], _ap[:, 512 * _sg:512 * _sg + 512]),
                "v" if sg == 3 else None)
        p.wait("V", last)
        tok_agg = p.op("V", lambda E, _s=slot: E.bn_aggr(
            AGG[:, 2 * _s:2 * _s + 2], BNS[:, 24 * _s:24 * _s + 24]), "v")
        p.wait("S", tok_agg)
        p.wait("S", tok_consts)
        tok_sd = p.op("S", lambda E, _s=slot: E.activation(
            out=AGG[:, 2 * _s + 1:2 * _s + 2], in_=AGG[:, 2 * _s + 1:2 * _s + 2],
            func=AF.Sqrt, bias=CONSTC[:, 0:1]), "s")
        p.wait("V", tok_sd)
        rs_ap = RSV[:, slot:slot + 1] if out is None else out[0]
        nm_ap = NMRS[:, slot:slot + 1] if out is None else out[1]
        tok_rc = p.op("V", lambda E, _s=slot, _r=rs_ap: E.reciprocal(
            _r, AGG[:, 2 * _s + 1:2 * _s + 2]), "v")
        p.wait("V", tok_rc)
        tok_tt = p.op("V", lambda E, _s=slot, _r=rs_ap, _n=nm_ap: E.tensor_tensor(
            _n, AGG[:, 2 * _s:2 * _s + 1], _r, ALU.mult), "v")
        p.wait("V", tok_tt)
        tok_f = p.op("V", lambda E, _n=nm_ap: E.tensor_scalar_mul(
            _n, _n, -1.0), "v")
        if out is not None:
            lnguard[slot] = tok_sd
        return tok_f

    for tc in range(NT):
        slot = tc % 2
        xs = U4[:, slot * 2048:(slot + 1) * 2048]
        tok_x = dma("Y", xs, x_d[tc * 128:(tc + 1) * 128, :], waits=[u4tok[slot]])
        tok_st = ln_stats(xs, slot, [tok_x])
        p.wait("S", tok_st)
        p.wait("S", lntm_free)
        tok_ap = p.op("S", lambda E, _xs=xs, _s=slot: E.activation(
            out=LNTM[:], in_=_xs, func=AF.Identity,
            bias=NMRS[:, _s:_s + 1], scale=RSV[:, _s:_s + 1]), "s")
        u4tok[slot] = tok_ap
        lnguard[slot] = tok_ap
        p.wait("T", tok_ap)
        p.wait("T", tok_id)
        burst_end = {}
        for dc in range(ND):
            hb = 4 + dc // 4
            sl = dc % 4
            if sl == 0:
                p.wait("T", bank[hb]["r"])
            tok_tr = p.op("T", lambda E, _hb=hb, _sl=sl, _dc=dc: E.transpose(
                PSAP[_hb][:, _sl * 128:_sl * 128 + 128],
                LNTM[:, _dc * 128:_dc * 128 + 128], IDENTF[:]),
                "t" if sl == 3 else None)
            if sl == 3:
                burst_end[hb] = tok_tr
                bank[hb]["w"] = tok_tr
        last_tr = tok_tr
        for dc in range(ND):
            hb = 4 + dc // 4
            sl = dc % 4
            p.wait("V", burst_end[hb])
            dst = U1[:, dc * 2048 + tc * 128: dc * 2048 + tc * 128 + 128]
            tev = p.op("V", lambda E, _d=dst, _hb=hb, _sl=sl: E.tensor_copy(
                _d, PSAP[_hb][:, _sl * 128:_sl * 128 + 128]), "v")
            if sl == 3:
                bank[hb]["r"] = tev
        lntm_free = last_tr
    ln1_ready = p.tok("v")

    if "ln1" in tap_d:
        dma("G", tap_d["ln1"], U1[:], waits=[ln1_ready], sem="do")
    if "lntm" in tap_d:
        dma("G", tap_d["lntm"], LNTM[:], waits=[ln1_ready, p.tok("s")], sem="do")
    if "stats" in tap_d:
        tt = [ln1_ready, p.tok("s")]
        dma("Y", tap_d["stats"][:, 0:48], BNS[:], waits=tt, sem="do")
        dma("Y", tap_d["stats"][:, 48:52], AGG[:], sem="do")
        dma("Y", tap_d["stats"][:, 52:54], SDV[:], sem="do")
        dma("Y", tap_d["stats"][:, 54:56], RSV[:], sem="do")
        dma("Y", tap_d["stats"][:, 56:58], NMRS[:], sem="do")

    WQ = U3[:, 0:2048]
    WK = U3[:, 2048:4096]
    WV = U3[:, 4096:6144]
    PSLOT = [CBUF[:, 1024:2048], CBUF[:, 2048:3072], CBUF[:, 3072:4096]]
    IREP = [CBUF[:, 4096:4608]]
    BVH = CBUF[0:1, 896:1024]

    w3_free = None
    pslot_free = [None, None, None]
    irep_free = [None]
    bvh_free = None
    deferred = []
    stat_tile_toks = []

    def wstat_tile(m):
        slot = m % 2
        xs = U4[:, slot * 2048:(slot + 1) * 2048]
        if m < 64:
            src = w1qin_d[m]
        else:
            fc = m - 64
            src = w2t_d[fc * 128:(fc + 1) * 128, :]
        tok_l = dma("Y", xs, src, waits=[u4tok[slot]])
        p.wait("V", tok_l)
        t = p.op("V", lambda E, _xs=xs, _m=m: E.reduce_sum(
            WSTATP[:, _m:_m + 1], _xs, axis=AX.X, apply_absolute_value=True), "v")
        u4tok[slot] = t
        stat_tile_toks.append(t)

    w1_store_tok = [None] * NF
    w2_store_tok = [[None] * NF for _ in range(4)]
    qq_store_prev = [None, None]
    qi_par = 0
    qi = 0
    thr_toks = [None, None]

    def quant_block(wi, fcb):
        nonlocal qi_par
        par = qi_par
        qi_par ^= 1
        slot = par
        xs = U4[:, slot * 2048:(slot + 1) * 2048]
        s1b = U5[:, (0 if par == 0 else 4096):(2048 if par == 0 else 6144)]
        s2b = U5[:, (2048 if par == 0 else 6144):(4096 if par == 0 else 8192)]
        if wi == 0:
            tok_l = dma("Y", xs, w1qin_d[fcb], waits=[u4tok[slot]])
        else:
            tok_l = dma("Y", xs, w2t_d[fcb * 128:(fcb + 1) * 128, :], waits=[u4tok[slot]])
        tn, tp = thr_toks[wi]
        p.wait("S", tok_l)
        p.wait("S", tn)
        p.wait("S", tp)
        p.wait("S", qq_store_prev[par])
        c0 = 2 * wi
        tok_s1 = p.op("S", lambda E, _x=xs, _c=c0, _o=s1b: E.activation(
            out=_o, in_=_x, func=AF.Sign, bias=THR[:, _c:_c + 1]), "s")
        tok_s2 = p.op("S", lambda E, _x=xs, _c=c0, _o=s2b: E.activation(
            out=_o, in_=_x, func=AF.Sign, bias=THR[:, _c + 1:_c + 2]), "s")
        u4tok[slot] = tok_s2
        p.wait("V", tok_s2)
        tok_q = p.op("V", lambda E, _a=s1b, _b2=s2b: E.tensor_tensor(
            _a, _a, _b2, ALU.add), "v")
        if wi == 0:
            t = dma("G", w1q8[fcb], s1b, waits=[tok_q])
            w1_store_tok[fcb] = t
        else:
            dstq = w2q8[:, fcb].rearrange("q p c -> p q c")
            t = dma("G", dstq, s1b, waits=[tok_q])
            for q in range(4):
                w2_store_tok[q][fcb] = t
        qq_store_prev[par] = t

    def emit_threshold(wi):
        col = 64 * wi
        p.wait("V", stat_tile_toks[-1])
        tok_r = p.op("V", lambda E, _c=col: E.reduce_sum(
            WSTATP[:, _c:_c + 1], WSTATP[:, _c:_c + 64], axis=AX.X), "v")
        p.wait("V", tok_r)
        b = 0
        p.wait("T", bank[b]["r"])
        p.wait("T", tok_consts)
        p.wait("T", tok_r)
        tok_m1 = p.op("T", lambda E, _b=b, _c=col: E.matmul(
            PSAP[_b][0:1, 0:1], WSTATP[:, _c:_c + 1], ONESCF2[:], start=True, stop=True), "t")
        bank[b]["w"] = tok_m1
        p.wait("S", tok_m1)
        tok_tc = p.op("S", lambda E, _b=b, _w=wi: E.copy(
            out=TOT[:, _w:_w + 1], in_=PSAP[_b][0:1, 0:1]), "s")
        bank[b]["r"] = tok_tc
        b2 = 1
        p.wait("T", tok_tc)
        p.wait("T", bank[b2]["r"])
        tok_m2 = p.op("T", lambda E, _b=b2, _w=wi: E.matmul(
            PSAP[_b][:, 0:1], ONES1F[:], TOT[:, _w:_w + 1], start=True, stop=True), "t")
        bank[b2]["w"] = tok_m2
        p.wait("S", tok_m2)
        cmul = 0.7 / NW1
        tn = p.op("S", lambda E, _b=b2, _w=wi, _c=cmul: E.mul(
            out=THR[:, 2 * _w:2 * _w + 1], in_=PSAP[_b][:, 0:1], mul=-_c), "s")
        tp = p.op("S", lambda E, _b=b2, _w=wi, _c=cmul: E.mul(
            out=THR[:, 2 * _w + 1:2 * _w + 2], in_=PSAP[_b][:, 0:1], mul=_c), "s")
        bank[b2]["r"] = tp
        thr_toks[wi] = (tn, tp)

    SWW = {0: SW0, 1: SW1, 2: SW2}
    for h in range(NHEAD):
        tok_w = dma("G", U3[:, 0:6144], wqkvh_d[h], waits=[w3_free])
        tok_bv = dma("G", BVH, bv_d[:, h * 128:(h + 1) * 128], waits=[bvh_free])

        ev = {"K": [None] * 4, "Q": [None] * 4}
        vev = [None] * NT
        rot = 0
        for name, coff, dstb, bcol in (("K", 128, KB, 16 + h), ("Q", 0, QB, h)):
            for tq in range(4):
                b = rot % 8
                rot += 1
                p.wait("T", bank[b]["r"])
                p.wait("T", tok_w)
                p.wait("T", ln1_ready)
                for dc in range(ND):
                    lhs = U3[:, dc * 384 + coff: dc * 384 + coff + 128]
                    rhs = U1[:, dc * 2048 + tq * 512: dc * 2048 + tq * 512 + 512]
                    tk = p.op("T", lambda E, _l=lhs, _r=rhs, _b=b, _dc=dc: E.matmul(
                        PSAP[_b], _l, _r, start=(_dc == 0), stop=(_dc == ND - 1)),
                        "t" if dc == ND - 1 else None)
                bank[b]["w"] = tk
                p.wait("S", tk)
                p.wait("S", tok_bq)
                dst = dstb[:, tq * 512:(tq + 1) * 512]
                te = p.op("S", lambda E, _d=dst, _b=b, _c=bcol: E.activation(
                    out=_d, in_=PSAP[_b], func=AF.Identity,
                    bias=BQS[:, _c:_c + 1]), "s")
                bank[b]["r"] = te
                ev[name][tq] = te
        for tc in range(NT):
            b = rot % 8
            rot += 1
            p.wait("T", bank[b]["r"])
            if tc == 0:
                p.wait("T", tok_bv)
                p.wait("T", tok_consts)
            for dc in range(ND):
                lhs = U1[:, dc * 2048 + tc * 128: dc * 2048 + tc * 128 + 128]
                rhs = U3[:, dc * 384 + 256: dc * 384 + 384]
                p.op("T", lambda E, _l=lhs, _r=rhs, _b=b, _dc=dc: E.matmul(
                    PSAP[_b][:, 0:128], _l, _r, start=(_dc == 0), stop=False))
            tk = p.op("T", lambda E, _b=b: E.matmul(
                PSAP[_b][:, 0:128], ONES1, BVH, start=False, stop=True), "t")
            bank[b]["w"] = tk
            p.wait("V", tk)
            dst = VB[:, tc * 128:(tc + 1) * 128]
            vev[tc] = p.op("V", lambda E, _d=dst, _b=b: E.tensor_copy(
                _d, PSAP[_b][:, 0:128]), "v")
            bank[b]["r"] = vev[tc]
        w3_free = p.tok("t")
        bvh_free = p.tok("t")

        if h == 0 and "qkv0" in tap_d:
            tk = p.tok("s")
            dma("G", tap_d["qkv0"][:, 0:2048], QB[:], waits=[tk, vev[NT - 1]], sem="do")
            dma("G", tap_d["qkv0"][:, 2048:4096], KB[:], sem="do")
            dma("G", tap_d["qkv0"][:, 4096:6144], VB[:], sem="do")

        p.wait("T", tok_mask)
        p.wait("T", tok_idb)
        for j in range(NQ):
            nk = 4 * j + 4
            npair = nk // 2
            p.wait("T", bank[0]["r"])
            p.wait("T", bank[1]["r"])
            qrhs = QB[:, j * 512:(j + 1) * 512]
            tok_grp = None
            exp_toks = [None] * npair
            pbufs = [None] * npair

            def emit_S(pr):
                wi = pr % 3
                SW = SWW[wi]
                bk0, bk1 = 2 + 2 * wi, 3 + 2 * wi
                kt0, kt1 = 2 * pr, 2 * pr + 1
                p.wait("T", bank[bk0]["r"])
                p.wait("T", bank[bk1]["r"])
                p.wait("T", ev["K"][kt1 // 4])
                p.wait("T", ev["Q"][j])
                tok_s = None
                for half, kt in ((0, kt0), (1, kt1)):
                    m = kt - 4 * j
                    sl = SW[:, half * 512:half * 512 + 512]
                    lhs = KB[:, kt * 128:kt * 128 + 128]
                    last = (half == 1 and m < 0)
                    tok_s = p.op("T", lambda E, _o=sl, _l=lhs, _r=qrhs, _m=m: E.matmul(
                        _o, _l, _r, start=True, stop=(_m < 0)),
                        "t" if last else None)
                    if m >= 0:
                        msk = CBUF[:, 384 - 128 * m:896 - 128 * m]
                        tok_s = p.op("T", lambda E, _o=sl, _msk=msk: E.matmul(
                            _o, IDENTB[:], _msk, start=False, stop=True),
                            "t" if half == 1 else None)
                bank[bk0]["w"] = tok_s
                bank[bk1]["w"] = tok_s
                p.wait("S", tok_s)
                p.wait("S", pslot_free[wi])
                pb = PSLOT[wi]
                tok_e = p.op("S", lambda E, _p=pb, _SW=SW: E.activation(
                    out=_p, in_=_SW[:], func=AF.Exp,
                    bias=CONSTC[:, 1:2], scale=ISQ), "s")
                bank[bk0]["r"] = tok_e
                bank[bk1]["r"] = tok_e
                exp_toks[pr] = tok_e
                pbufs[pr] = pb

            def emit_PV(pr):
                nonlocal tok_grp
                wi = pr % 3
                kt0, kt1 = 2 * pr, 2 * pr + 1
                pb = pbufs[pr]
                p.wait("T", exp_toks[pr])
                p.wait("T", vev[kt1])
                for half, kt in ((0, kt0), (1, kt1)):
                    lhsv = VB[:, kt * 128:kt * 128 + 128]
                    prh = pb[:, half * 512:half * 512 + 512]
                    p.op("T", lambda E, _l=lhsv, _p=prh, _st=(pr == 0 and half == 0),
                         _sp=(pr == npair - 1 and half == 1): E.matmul(
                        PO[:], _l, _p, start=_st, stop=_sp))
                    tok_grp = p.op("T", lambda E, _p=prh, _st=(pr == 0 and half == 0),
                                   _sp=(pr == npair - 1 and half == 1): E.matmul(
                        PDEN[0:1, :], ONESCOLB[:], _p, start=_st, stop=_sp),
                        "t")
                pslot_free[wi] = tok_grp

            for pr in range(npair):
                emit_S(pr)
                if pr >= 2:
                    emit_PV(pr - 2)
            emit_PV(npair - 2) if npair >= 2 else None
            emit_PV(npair - 1)
            bank[0]["w"] = tok_grp
            bank[1]["w"] = tok_grp
            dst = U2[:, h * 2048 + j * 512: h * 2048 + (j + 1) * 512]
            p.wait("V", tok_grp)
            tok_oc = p.op("V", lambda E, _d=dst: E.tensor_copy(_d, PO[:]), "v")
            bank[0]["r"] = tok_oc
            invf_j = LNTM[0:1, j * 512:(j + 1) * 512]
            tok_rc = p.op("V", lambda E, _i=invf_j: E.reciprocal(_i, PDEN[0:1, :]), "v")
            bank[1]["r"] = tok_rc
            deferred.append((dst, invf_j, tok_rc))

        for dst, invf_j, tok_rc in deferred:
            rb = 6
            p.wait("T", tok_rc)
            p.wait("T", bank[rb]["r"])
            tok_rep = p.op("T", lambda E, _b=rb, _i=invf_j: E.matmul(
                PSAP[_b], ONES1F[:], _i, start=True, stop=True), "t")
            bank[rb]["w"] = tok_rep
            p.wait("S", tok_rep)
            p.wait("S", irep_free[0])
            tok_ir = p.op("S", lambda E, _b=rb, _i=IREP[0]: E.activation(
                out=_i, in_=PSAP[_b], func=AF.Copy), "s")
            bank[rb]["r"] = tok_ir
            p.wait("V", tok_ir)
            tok_on = p.op("V", lambda E, _d=dst, _i=IREP[0]: E.tensor_tensor(
                _d, _d, _i, ALU.mult), "v")
            irep_free[0] = tok_on
        deferred = []

        for m in range(8 * h, 8 * h + 8):
            wstat_tile(m)
        if h == 15:
            emit_threshold(0)
            emit_threshold(1)

    o_ready = p.tok("v")
    if "o" in tap_d:
        dma("G", tap_d["o"], U2[:], waits=[o_ready], sem="do")

    if "thr" in tap_d:
        dma("Y", tap_d["thr"], THR[:], waits=[thr_toks[1][1]], sem="do")

    BOROW = CBUF[0:1, 1024:3072]
    tok_bo = dma("G", BOROW, bo_d, waits=[p.tok("t")])
    wo_free = w3_free
    proj_guard = p.tok("t")
    u2r_guard = None
    last_wo_mm = None
    proj_ev = [[None] * 4 for _ in range(NT)]

    cq = 0
    for ds in range(4):
        src = wo_d[:, ds * 512:(ds + 1) * 512].rearrange("(a p) c -> p a c", p=128)
        dst = U3[:].rearrange("p (a c) -> p a c", c=512)
        tok_wo = dma("G", dst, src, waits=[wo_free])
        for tc in range(NT):
            b = (ds * NT + tc) % 8
            p.wait("T", bank[b]["r"])
            p.wait("T", tok_wo)
            p.wait("T", o_ready)
            if ds == 0 and tc == 0:
                p.wait("T", tok_bo)
            for dc in range(ND):
                lhs = U2[:, dc * 2048 + tc * 128: dc * 2048 + tc * 128 + 128]
                rhs = U3[:, dc * 512:(dc + 1) * 512]
                p.op("T", lambda E, _l=lhs, _r=rhs, _b=b, _dc=dc: E.matmul(
                    PSAP[_b], _l, _r, start=(_dc == 0), stop=False))
            tk = p.op("T", lambda E, _b=b, _ds=ds: E.matmul(
                PSAP[_b], ONES1, BOROW[0:1, _ds * 512:(_ds + 1) * 512],
                start=False, stop=True), "t")
            bank[b]["w"] = tk
            last_wo_mm = tk
            p.wait("S", tk)
            p.wait("S", proj_guard)
            dst2 = U1[:, tc * 2048 + ds * 512: tc * 2048 + (ds + 1) * 512]
            te = p.op("S", lambda E, _d=dst2, _b=b: E.activation(
                out=_d, in_=PSAP[_b], func=AF.Copy), "s")
            bank[b]["r"] = te
            proj_ev[tc][ds] = te
            for _ in range(2):
                if cq < 128:
                    wi, fcb = (0, cq) if cq < 64 else (1, cq - 64)
                    quant_block(wi, fcb); cq += 1

        wo_free = last_wo_mm
    u2r_guard = last_wo_mm
    if "proj" in tap_d:
        dma("G", tap_d["proj"], U1[:], waits=[p.tok("s"), p.tok("v")], sem="do")

    u5_free = [None, None]
    y_toks = []
    ln2st_tok = [None] * NT
    for tc in range(NT):
        slot = tc % 2
        xs = U4[:, slot * 2048:(slot + 1) * 2048]
        tok_x = dma("Y", xs, x_d[tc * 128:(tc + 1) * 128, :], waits=[u4tok[slot]])
        proj_sl = U1[:, tc * 2048:(tc + 1) * 2048]
        tok_st = ln_stats(proj_sl, slot, [tok_x] + proj_ev[tc])
        p.wait("V", tok_st)
        p.wait("V", u5_free[slot])
        lnv = U5[:, slot * 2048:(slot + 1) * 2048]
        tok_ln = p.op("V", lambda E, _p=proj_sl, _l=lnv, _s=slot: E.tensor_scalar(
            _l, _p, RSV[:, _s:_s + 1], NMRS[:, _s:_s + 1],
            ALU.mult, ALU.add), "v")
        lnguard[slot] = tok_ln
        p.wait("V", tok_ln)
        p.wait("V", tok_x)
        p.wait("V", u2r_guard)
        ydst = U2[:, tc * 2048:(tc + 1) * 2048]
        ty = p.op("V", lambda E, _y=ydst, _x=xs, _l=lnv: E.tensor_tensor(
            _y, _x, _l, ALU.add), "v")
        u4tok[slot] = ty
        u5_free[slot] = ty
        y_toks.append(ty)
        ln2st_tok[tc] = ln_stats(ydst, slot, [ty],
                                 out=(WSTATP[:, 2 * tc:2 * tc + 1],
                                      WSTATP[:, 2 * tc + 1:2 * tc + 2]))

    y_ready = p.tok("v")
    if "y" in tap_d:
        dma("G", tap_d["y"], U2[:], waits=[y_ready], sem="do")

    B2ROW = CBUF[0:1, 2560:4608]
    tok_b2l = dma("G", B2ROW, b2_d,
                  waits=[last_wo_mm, p.tok("t"), qq_store_prev[0], qq_store_prev[1]])
    p.wait("S", tok_b2l)
    tok_b2 = p.op("S", lambda E: E.mul(out=B2ROW, in_=B2ROW, mul=2.0), "s")
    WSL = [U3[:, i * 2048:(i + 1) * 2048] for i in range(4)]
    u3d_free = [wo_free, wo_free, wo_free, wo_free]
    PR2 = [KB, QB, VB, CBUF[:, 0:2048]]
    pr2_free = [p.tok("t")] * 4
    u5w_free = p.tok("v")
    lntm_free = p.tok("t")
    g_guard = None
    u1d_guard = p.tok("v")
    u1d_guard_s = p.tok("s")

    def emit_ln2prep(w):
        last = None
        for tsub in range(4):
            tc = 4 * w + tsub
            ysl = U2[:, tc * 2048:(tc + 1) * 2048]
            stage = U4[:, 0:2048]
            p.wait("S", ln2st_tok[tc])
            p.wait("S", u4tok[0])
            tok_ap = p.op("S", lambda E, _y=ysl, _tc=tc, _o=stage: E.activation(
                out=_o, in_=_y, func=AF.Identity,
                bias=WSTATP[:, 2 * _tc + 1:2 * _tc + 2],
                scale=WSTATP[:, 2 * _tc:2 * _tc + 1]), "s")
            p.wait("T", tok_ap)
            for qtr in range(4):
                hb = 6 + qtr % 2
                p.wait("T", bank[hb]["r"])
                for sl in range(4):
                    dc = qtr * 4 + sl
                    tok_tr = p.op("T", lambda E, _hb=hb, _sl=sl, _dc=dc, _i=stage: E.transpose(
                        PSAP[_hb][:, _sl * 128:_sl * 128 + 128],
                        _i[:, _dc * 128:_dc * 128 + 128], IDENTF[:]),
                        "t" if sl == 3 else None)
                bank[hb]["w"] = tok_tr
                p.wait("V", tok_tr)
                for sl in range(4):
                    dc = qtr * 4 + sl
                    p.wait("V", u5w_free)
                    dst = U5[:, dc * 512 + tsub * 128: dc * 512 + tsub * 128 + 128]
                    tev = p.op("V", lambda E, _d=dst, _hb=hb, _sl=sl: E.tensor_copy(
                        _d, PSAP[_hb][:, _sl * 128:_sl * 128 + 128]), "v")
                    if sl == 3:
                        bank[hb]["r"] = tev
                    last = tev
            u4tok[0] = tok_tr
        return last

    p.wait("V", qq_store_prev[0])
    p.wait("V", qq_store_prev[1])
    ln2_ready_arr = [None] * 4
    ln2_ready_arr[0] = emit_ln2prep(0)
    for w in range(4):
        ln2_ready = ln2_ready_arr[w]

        f1ev = [None] * NF
        for fc in range(NF):
            slot = fc % 4
            tok_wl = dma("G", WSL[slot], w1q8[fc],
                         waits=[u3d_free[slot], w1_store_tok[fc]])
            b = fc % 2
            p.wait("T", bank[b]["r"])
            p.wait("T", tok_wl)
            p.wait("T", ln2_ready)
            for dc in range(ND):
                lhs = WSL[slot][:, dc * 128:dc * 128 + 128]
                rhs = U5[:, dc * 512:(dc + 1) * 512]
                tk = p.op("T", lambda E, _l=lhs, _r=rhs, _b=b, _dc=dc: E.matmul(
                    PSAP[_b], _l, _r, start=(_dc == 0), stop=(_dc == ND - 1)),
                    "t" if dc == ND - 1 else None)
            bank[b]["w"] = tk
            u3d_free[slot] = tk
            p.wait("S", tk)
            if w == 0:
                p.wait("S", u1d_guard_s)
                p.wait("S", u1d_guard)
                if fc == 0:
                    p.wait("S", tok_b1)
            p.wait("S", g_guard)
            dst = U1[:, fc * 512:(fc + 1) * 512]
            te = p.op("S", lambda E, _d=dst, _b=b, _fc=fc: E.activation(
                out=_d, in_=PSAP[_b], func=AF.Gelu,
                bias=B1S[:, _fc:_fc + 1], scale=0.5), "s")
            bank[b]["r"] = te
            f1ev[fc] = te
        if w == 0 and "g0" in tap_d:
            dma("G", tap_d["g0"], U1[:], waits=[p.tok("s")], sem="do")

        proj2_ev = [[None] * 4 for _ in range(4)]
        for q in range(4):
            last_grp = None
            for fcg in range(16):
                slot = fcg % 4
                src = w2q8[q, fcg * 4:fcg * 4 + 4].rearrange("a p c -> p a c")
                dst = WSL[slot].rearrange("p (a c) -> p a c", c=512)
                tok_wl = dma("G", dst, src,
                             waits=[u3d_free[slot], w2_store_tok[q][fcg * 4 + 3]])
                p.wait("T", tok_wl)
                for fc in range(fcg * 4, fcg * 4 + 4):
                    p.wait("T", f1ev[fc])
                    for tsub in range(4):
                        b = 2 + tsub
                        if fc == 0:
                            p.wait("T", bank[b]["r"])
                        last = (fc == fcg * 4 + 3 and tsub == 3)
                        lhs = U1[:, fc * 512 + tsub * 128: fc * 512 + tsub * 128 + 128]
                        rhs = WSL[slot][:, (fc % 4) * 512:(fc % 4) * 512 + 512]
                        last_grp = p.op("T", lambda E, _l=lhs, _r=rhs, _b=b, _fc=fc: E.matmul(
                            PSAP[_b], _l, _r, start=(_fc == 0), stop=False),
                            "t" if last else None)
                u3d_free[slot] = last_grp
            for tsub in range(4):
                b = 2 + tsub
                if q == 0 and tsub == 0:
                    p.wait("T", tok_b2)
                tk = p.op("T", lambda E, _b=b, _q=q: E.matmul(
                    PSAP[_b], ONES1, B2ROW[0:1, _q * 512:(_q + 1) * 512],
                    start=False, stop=True), "t")
                bank[b]["w"] = tk
                p.wait("S", tk)
                p.wait("S", pr2_free[tsub])
                dst = PR2[tsub][:, q * 512:(q + 1) * 512]
                te = p.op("S", lambda E, _d=dst, _b=b: E.activation(
                    out=_d, in_=PSAP[_b], func=AF.Copy, scale=0.5), "s")
                bank[b]["r"] = te
                proj2_ev[tsub][q] = te

        g_guard = p.tok("t")

        u5w_free = f1ev[-1]
        if w < 3:
            ln2_ready_arr[w + 1] = emit_ln2prep(w + 1)

        out_toks = []
        for tsub in range(4):
            tc = 4 * w + tsub
            slot = tsub % 2
            p2 = PR2[tsub][:, 0:2048]
            tok_st = ln_stats(p2, slot, proj2_ev[tsub])
            p.wait("V", tok_st)
            p.wait("V", lntm_free)
            tok_ln = p.op("V", lambda E, _p=p2, _s=slot: E.tensor_scalar(
                LNTM[:], _p, RSV[:, _s:_s + 1], NMRS[:, _s:_s + 1],
                ALU.mult, ALU.add), "v")
            lnguard[slot] = tok_ln
            pr2_free[tsub] = tok_ln
            xs = U4[:, 2048:4096]
            p.wait("V", tok_ln)
            p.wait("V", u4tok[1])
            ysl = U2[:, tc * 2048:(tc + 1) * 2048]
            tv = p.op("V", lambda E, _o=xs, _y=ysl, _l=LNTM: E.tensor_tensor(
                _o, _y, _l[:], ALU.add), "v")
            lntm_free = tv
            td = dma("Y", out_d[tc * 128:(tc + 1) * 128, :], xs, waits=[tv], sem="do")
            u4tok[1] = td
            out_toks.append(td)

    p.wait("Y", p.tok("do"))

    with nc.Block() as block:
        @block.tensor
        def _(E):
            for fn in p.fns["T"]:
                fn(E)

        @block.vector
        def _(E):
            for fn in p.fns["V"]:
                fn(E)

        @block.scalar
        def _(E):
            for fn in p.fns["S"]:
                fn(E)

        @block.sync
        def _(E):
            for fn in p.fns["Y"]:
                fn(E)

        @block.gpsimd
        def _(E):
            for fn in p.fns["G"]:
                fn(E)

    es.close()
    return nc


def make_inputs(inputs, core):
    x = np.asarray(inputs["L0_attn_x"], dtype=np.float32)
    wqkv = np.ascontiguousarray(np.asarray(inputs["L0_attn_Wqkv"], dtype=np.float32))
    bqkv = np.asarray(inputs["L0_attn_bqkv"], dtype=np.float32)
    wo = np.ascontiguousarray(np.asarray(inputs["L0_attn_Wo"], dtype=np.float32))
    bo = np.asarray(inputs["L0_attn_bo"], dtype=np.float32)
    w1 = np.asarray(inputs["L0_ffn_w1"], dtype=np.float32)
    b1 = np.asarray(inputs["L0_ffn_b1"], dtype=np.float32)
    w2 = np.asarray(inputs["L0_ffn_w2"], dtype=np.float32)
    b2 = np.asarray(inputs["L0_ffn_b2"], dtype=np.float32)
    p = np.arange(128)[:, None]
    j = np.arange(896)[None, :]
    maskm = np.where(j >= p + 384, 0.0, -30000.0).astype(np.float32)
    W4 = wqkv.reshape(16, 128, 3, 16, 128)
    wqkvh = np.ascontiguousarray(W4.transpose(3, 1, 0, 2, 4).reshape(16, 128, 6144))
    common = {
        "wqkvh": wqkvh,
        "wo": wo,
        "w1qin": np.ascontiguousarray(w1.reshape(64, 128, 16, 128).transpose(0, 3, 2, 1).reshape(64, 128, 2048)),
        "w2t": np.ascontiguousarray(w2.T),
        "bq": np.ascontiguousarray(bqkv.reshape(48, 128)),
        "bvrow": np.ascontiguousarray(bqkv[4096:].reshape(1, 2048)),
        "borow": np.ascontiguousarray(bo.reshape(1, 2048)),
        "b1": np.ascontiguousarray(b1.reshape(64, 128)),
        "b2row": np.ascontiguousarray(b2.reshape(1, 2048)),
        "maskm": maskm,
        "ident": np.eye(128, dtype=np.float32),
    }
    return {"x": np.ascontiguousarray(x[core]), **common}


def kernel(**inputs):
    nc = build()
    in_maps = [make_inputs(inputs, c) for c in range(8)]
    res = run_bass_kernel_spmd(nc, in_maps, core_ids=list(range(8)))
    return np.stack([np.asarray(res.results[c]["out"]) for c in range(8)], axis=0)


# version 14
# speedup vs baseline: 1.0407x; 1.0407x over previous
import math
import contextlib
import numpy as np
import concourse.bass as bass
import concourse.mybir as mybir
from concourse.bass_utils import run_bass_kernel_spmd

F32 = mybir.dt.float32
BF16 = mybir.dt.bfloat16
FP8 = mybir.dt.float8e4
AF = mybir.ActivationFunctionType
ALU = mybir.AluOpType
AX = mybir.AxisListType

SEQ = 2048
DM = 2048
NHEAD = 16
HD = 128
FF = 8192
NT = SEQ // 128
ND = DM // 128
NQ = SEQ // 512
NF = FF // 128
ISQ = 1.0 / math.sqrt(HD)
ESHIFT = -12.0
EPS = 1e-5
NW1 = DM * FF

ENGS = ("T", "V", "S", "Y", "G")
DMA_SEMS = ("dy", "dg", "do")


class Plan:
    def __init__(self):
        self.fns = {e: [] for e in ENGS}
        self.cnt = {s: 0 for s in ("t", "v", "s") + DMA_SEMS}
        self.lw = {e: {} for e in ENGS}
        self.sem = {}

    def raw(self, eng, fn):
        self.fns[eng].append(fn)

    def wait(self, eng, tok):
        if tok is None:
            return
        s, v = tok
        if v <= 0:
            return
        assert v <= self.cnt[s], f"wait-future {eng} {s} {v}>{self.cnt[s]}"
        if self.lw[eng].get(s, 0) >= v:
            return
        self.lw[eng][s] = v
        h = self.sem[s]
        self.raw(eng, lambda E, _h=h, _v=v: E.wait_ge(_h, _v))

    def op(self, eng, fn, sem=None, k=1):
        if sem is None:
            self.raw(eng, fn)
            return None
        self.cnt[sem] += k
        v = self.cnt[sem]
        h = self.sem[sem]
        self.raw(eng, lambda E, _f=fn, _h=h, _k=k: _f(E).then_inc(_h, _k))
        return (sem, v)

    def tok(self, sem):
        return (sem, self.cnt[sem])


def build(taps=()):
    nc = bass.Bass("TRN2", target_bir_lowering=False)

    x_d = nc.dram_tensor("x", [SEQ, DM], F32, kind="ExternalInput").ap()
    wqkvh_d = nc.dram_tensor("wqkvh", [16, 128, 6144], F32, kind="ExternalInput").ap()
    wo_d = nc.dram_tensor("wo", [DM, DM], F32, kind="ExternalInput").ap()
    w1qin_d = nc.dram_tensor("w1qin", [NF, 128, 2048], F32, kind="ExternalInput").ap()
    w2t_d = nc.dram_tensor("w2t", [FF, DM], F32, kind="ExternalInput").ap()
    bq_d = nc.dram_tensor("bq", [48, 128], F32, kind="ExternalInput").ap()
    bv_d = nc.dram_tensor("bvrow", [1, DM], F32, kind="ExternalInput").ap()
    bo_d = nc.dram_tensor("borow", [1, DM], F32, kind="ExternalInput").ap()
    b1_d = nc.dram_tensor("b1", [64, 128], F32, kind="ExternalInput").ap()
    b2_d = nc.dram_tensor("b2row", [1, DM], F32, kind="ExternalInput").ap()
    mm_d = nc.dram_tensor("maskm", [128, 896], F32, kind="ExternalInput").ap()
    id_d = nc.dram_tensor("ident", [128, 128], F32, kind="ExternalInput").ap()
    out_d = nc.dram_tensor("out", [SEQ, DM], F32, kind="ExternalOutput").ap()
    w1q8 = nc.dram_tensor("w1q8", [NF, 128, DM], FP8, kind="Internal").ap()
    w2q8 = nc.dram_tensor("w2q8", [4, NF, 128, 512], FP8, kind="Internal").ap()
    tap_d = {}
    for tname, shape in dict(
        ln1=[128, 32768], lntm=[128, 2048], stats=[128, 64], qkv0=[128, 3 * 2048], o=[128, 32768],
        proj=[128, 32768], y=[128, 32768], g0=[128, 32768],
        thr=[128, 4], den0=[1, 512],
    ).items():
        if tname in taps:
            tap_d[tname] = nc.dram_tensor("tap_" + tname, shape, F32,
                                          kind="ExternalOutput").ap()

    es = contextlib.ExitStack()
    sb = lambda name, shape, dt: es.enter_context(nc.sbuf_tensor(name, shape, dt))
    psb = lambda name, shape, dt: es.enter_context(nc.psum_tensor(name, shape, dt))

    U1 = sb("U1", [128, 32768], BF16)
    U2 = sb("U2", [128, 32768], BF16)
    U3 = sb("U3", [128, 8192], BF16)
    U4 = sb("U4", [128, 4096], F32)
    U5 = sb("U5", [128, 8192], BF16)
    LNTM = sb("LNTM", [128, 2048], F32)
    CBUF = sb("CBUF", [128, 4608], BF16)
    KB = sb("KB", [128, 2048], BF16)
    QB = sb("QB", [128, 2048], BF16)
    VB = sb("VB", [128, 2048], BF16)
    IDENTF = sb("IDENTF", [128, 128], F32)
    IDENTB = sb("IDENTB", [128, 128], BF16)
    WSTATP = sb("WSTATP", [128, 128], F32)
    BQS = sb("BQS", [128, 48], F32)
    B1S = sb("B1S", [128, 64], F32)
    BNS = sb("BNS", [128, 48], F32)
    AGG = sb("AGG", [128, 4], F32)
    RSV = sb("RSV", [128, 2], F32)
    NMRS = sb("NMRS", [128, 2], F32)
    CONSTC = sb("CONSTC", [128, 2], F32)
    THR = sb("THR", [128, 4], F32)
    ONES1T = sb("ONES1T", [1, 128], BF16)
    ONESCOLB = sb("ONESCOLB", [128, 1], BF16)
    ONES1F = sb("ONES1F", [1, 128], F32)
    ONESCF2 = sb("ONESCF2", [128, 1], F32)
    TOT = LNTM[0:1, 600:602]

    SW0 = psb("sw0", [128, 1024], F32)
    SW1 = psb("sw1", [128, 1024], F32)
    SW2 = psb("sw2", [128, 1024], F32)
    PO = psb("po", [128, 512], F32)
    PDEN = psb("pden", [128, 512], F32)
    PSAP = {0: PO[:], 1: PDEN[:], 2: SW0[:, 0:512], 3: SW0[:, 512:1024],
            4: SW1[:, 0:512], 5: SW1[:, 512:1024],
            6: SW2[:, 0:512], 7: SW2[:, 512:1024]}

    p = Plan()
    for s in p.cnt:
        p.sem[s] = es.enter_context(nc.semaphore("sem_" + s))

    ENG_DMA = {"Y": "dy", "G": "dg"}

    def dma(eng, out, in_, waits=(), sem=None):
        for w in waits:
            p.wait(eng, w)
        sem = sem or ENG_DMA[eng]
        return p.op(eng, lambda E, _o=out, _i=in_: E.dma_start(out=_o, in_=_i), sem, 16)

    def dma_nc(eng, out, in_, waits=(), sem=None):
        for w in waits:
            p.wait(eng, w)
        sem = sem or ENG_DMA[eng]
        return p.op(eng, lambda E, _o=out, _i=in_: E.dma_start(
            out=_o, in_=_i, allow_slow_non_contiguous=True), sem, 16)

    bank = [{"w": None, "r": None} for _ in range(8)]
    PSA = lambda b: PSAP[b]
    ptab = {"A": {"w": [None] * 8, "r": [None] * 8},
            "B": {"w": [None] * 8, "r": [None] * 8}}

    tok_mask = dma("G", CBUF[:, 0:896], mm_d)
    tok_id = dma("Y", IDENTF[:], id_d)
    tok_idb = dma("G", IDENTB[:], id_d)
    tok_bq = dma_nc("Y", BQS[:], bq_d.rearrange("c p -> p c"))
    tok_b1 = dma_nc("Y", B1S[:], b1_d.rearrange("c p -> p c"))
    p.op("V", lambda E: E.memset(CONSTC[:, 0:1], EPS))
    p.op("V", lambda E: E.memset(CONSTC[:, 1:2], ESHIFT))
    p.op("V", lambda E: E.memset(ONES1T[:], 1.0))
    p.op("V", lambda E: E.memset(ONESCOLB[:], 1.0))
    p.op("V", lambda E: E.memset(ONESCF2[:], 1.0))
    tok_consts = p.op("V", lambda E: E.memset(ONES1F[:], 1.0), "v")
    ONES1 = ONES1T[0:1, :]

    u4tok = [None, None]
    lntm_free = None
    lnguard = [None, None]

    def ln_stats(eng_src_ap, slot, wait_toks, out=None):
        for w in wait_toks:
            p.wait("V", w)
        p.wait("V", lnguard[slot])
        last = None
        for sg in range(4):
            a, b = slot * 24 + 6 * sg, slot * 24 + 6 * sg + 6
            last = p.op("V", lambda E, _a=a, _b=b, _sg=sg, _ap=eng_src_ap: E.bn_stats(
                BNS[:, _a:_b], _ap[:, 512 * _sg:512 * _sg + 512]),
                "v" if sg == 3 else None)
        p.wait("V", last)
        tok_agg = p.op("V", lambda E, _s=slot: E.bn_aggr(
            AGG[:, 2 * _s:2 * _s + 2], BNS[:, 24 * _s:24 * _s + 24]), "v")
        p.wait("S", tok_agg)
        p.wait("S", tok_consts)
        tok_sd = p.op("S", lambda E, _s=slot: E.activation(
            out=AGG[:, 2 * _s + 1:2 * _s + 2], in_=AGG[:, 2 * _s + 1:2 * _s + 2],
            func=AF.Sqrt, bias=CONSTC[:, 0:1]), "s")
        p.wait("V", tok_sd)
        rs_ap = RSV[:, slot:slot + 1] if out is None else out[0]
        nm_ap = NMRS[:, slot:slot + 1] if out is None else out[1]
        tok_rc = p.op("V", lambda E, _s=slot, _r=rs_ap: E.reciprocal(
            _r, AGG[:, 2 * _s + 1:2 * _s + 2]), "v")
        p.wait("V", tok_rc)
        tok_tt = p.op("V", lambda E, _s=slot, _r=rs_ap, _n=nm_ap: E.tensor_tensor(
            _n, AGG[:, 2 * _s:2 * _s + 1], _r, ALU.mult), "v")
        p.wait("V", tok_tt)
        tok_f = p.op("V", lambda E, _n=nm_ap: E.tensor_scalar_mul(
            _n, _n, -1.0), "v")
        if out is not None:
            lnguard[slot] = tok_sd
        return tok_f

    for tc in range(NT):
        slot = tc % 2
        xs = U4[:, slot * 2048:(slot + 1) * 2048]
        tok_x = dma("Y", xs, x_d[tc * 128:(tc + 1) * 128, :], waits=[u4tok[slot]])
        tok_st = ln_stats(xs, slot, [tok_x])
        p.wait("S", tok_st)
        p.wait("S", lntm_free)
        tok_ap = p.op("S", lambda E, _xs=xs, _s=slot: E.activation(
            out=LNTM[:], in_=_xs, func=AF.Identity,
            bias=NMRS[:, _s:_s + 1], scale=RSV[:, _s:_s + 1]), "s")
        u4tok[slot] = tok_ap
        lnguard[slot] = tok_ap
        p.wait("T", tok_ap)
        p.wait("T", tok_id)
        burst_end = {}
        for dc in range(ND):
            hb = 4 + dc // 4
            sl = dc % 4
            if sl == 0:
                p.wait("T", bank[hb]["r"])
            tok_tr = p.op("T", lambda E, _hb=hb, _sl=sl, _dc=dc: E.transpose(
                PSAP[_hb][:, _sl * 128:_sl * 128 + 128],
                LNTM[:, _dc * 128:_dc * 128 + 128], IDENTF[:]),
                "t" if sl == 3 else None)
            if sl == 3:
                burst_end[hb] = tok_tr
                bank[hb]["w"] = tok_tr
        last_tr = tok_tr
        for dc in range(ND):
            hb = 4 + dc // 4
            sl = dc % 4
            p.wait("V", burst_end[hb])
            dst = U1[:, dc * 2048 + tc * 128: dc * 2048 + tc * 128 + 128]
            tev = p.op("V", lambda E, _d=dst, _hb=hb, _sl=sl: E.tensor_copy(
                _d, PSAP[_hb][:, _sl * 128:_sl * 128 + 128]), "v")
            if sl == 3:
                bank[hb]["r"] = tev
        lntm_free = last_tr
    ln1_ready = p.tok("v")

    if "ln1" in tap_d:
        dma("G", tap_d["ln1"], U1[:], waits=[ln1_ready], sem="do")
    if "lntm" in tap_d:
        dma("G", tap_d["lntm"], LNTM[:], waits=[ln1_ready, p.tok("s")], sem="do")
    if "stats" in tap_d:
        tt = [ln1_ready, p.tok("s")]
        dma("Y", tap_d["stats"][:, 0:48], BNS[:], waits=tt, sem="do")
        dma("Y", tap_d["stats"][:, 48:52], AGG[:], sem="do")
        dma("Y", tap_d["stats"][:, 52:54], SDV[:], sem="do")
        dma("Y", tap_d["stats"][:, 54:56], RSV[:], sem="do")
        dma("Y", tap_d["stats"][:, 56:58], NMRS[:], sem="do")

    WQ = U3[:, 0:2048]
    WK = U3[:, 2048:4096]
    WV = U3[:, 4096:6144]
    PSLOT = [CBUF[:, 1024:2048], CBUF[:, 2048:3072], CBUF[:, 3072:4096]]
    IREP = [CBUF[:, 4096:4608]]
    BVH = CBUF[0:1, 896:1024]

    w3_free = None
    pslot_free = [None, None, None]
    irep_free = [None]
    bvh_free = None
    deferred = []
    stat_tile_toks = []

    def wstat_tile(m):
        slot = m % 2
        xs = U4[:, slot * 2048:(slot + 1) * 2048]
        if m < 64:
            src = w1qin_d[m]
        else:
            fc = m - 64
            src = w2t_d[fc * 128:(fc + 1) * 128, :]
        tok_l = dma("Y", xs, src, waits=[u4tok[slot]])
        p.wait("V", tok_l)
        t = p.op("V", lambda E, _xs=xs, _m=m: E.reduce_sum(
            WSTATP[:, _m:_m + 1], _xs, axis=AX.X, apply_absolute_value=True), "v")
        u4tok[slot] = t
        stat_tile_toks.append(t)

    w1_store_tok = [None] * NF
    w2_store_tok = [[None] * NF for _ in range(4)]
    qq_store_prev = [None, None, None, None]
    qi_par = 0
    qi = 0
    thr_toks = [None, None]

    QS1B = [U5[:, 0:2048], U5[:, 4096:6144], KB[:, 0:2048], VB[:, 0:2048]]
    QS2B = [U5[:, 2048:4096], U5[:, 6144:8192], QB[:, 0:2048], CBUF[:, 2560:4608]]

    def quant_block(wi, fcb):
        nonlocal qi_par
        par = qi_par
        qi_par = (qi_par + 1) % 4
        slot = par % 2
        xs = U4[:, slot * 2048:(slot + 1) * 2048]
        s1b = QS1B[par]
        s2b = QS2B[par]
        if wi == 0:
            tok_l = dma("Y", xs, w1qin_d[fcb], waits=[u4tok[slot]])
        else:
            tok_l = dma("Y", xs, w2t_d[fcb * 128:(fcb + 1) * 128, :], waits=[u4tok[slot]])
        tn, tp = thr_toks[wi]
        p.wait("S", tok_l)
        p.wait("S", tn)
        p.wait("S", tp)
        p.wait("S", qq_store_prev[par])
        p.wait("S", qsc_guard)
        p.wait("S", qsc_guard_v)
        c0 = 2 * wi
        tok_s1 = p.op("S", lambda E, _x=xs, _c=c0, _o=s1b: E.activation(
            out=_o, in_=_x, func=AF.Sign, bias=THR[:, _c:_c + 1]), "s")
        tok_s2 = p.op("S", lambda E, _x=xs, _c=c0, _o=s2b: E.activation(
            out=_o, in_=_x, func=AF.Sign, bias=THR[:, _c + 1:_c + 2]), "s")
        u4tok[slot] = tok_s2
        p.wait("V", tok_s2)
        tok_q = p.op("V", lambda E, _a=s1b, _b2=s2b: E.tensor_tensor(
            _a, _a, _b2, ALU.add), "v")
        if wi == 0:
            t = dma("G", w1q8[fcb], s1b, waits=[tok_q])
            w1_store_tok[fcb] = t
        else:
            dstq = w2q8[:, fcb].rearrange("q p c -> p q c")
            t = dma("G", dstq, s1b, waits=[tok_q])
            for q in range(4):
                w2_store_tok[q][fcb] = t
        qq_store_prev[par] = t

    def emit_threshold(wi):
        col = 64 * wi
        p.wait("V", stat_tile_toks[-1])
        tok_r = p.op("V", lambda E, _c=col: E.reduce_sum(
            WSTATP[:, _c:_c + 1], WSTATP[:, _c:_c + 64], axis=AX.X), "v")
        p.wait("V", tok_r)
        b = 0
        p.wait("T", bank[b]["r"])
        p.wait("T", tok_consts)
        p.wait("T", tok_r)
        tok_m1 = p.op("T", lambda E, _b=b, _c=col: E.matmul(
            PSAP[_b][0:1, 0:1], WSTATP[:, _c:_c + 1], ONESCF2[:], start=True, stop=True), "t")
        bank[b]["w"] = tok_m1
        p.wait("S", tok_m1)
        tok_tc = p.op("S", lambda E, _b=b, _w=wi: E.copy(
            out=TOT[:, _w:_w + 1], in_=PSAP[_b][0:1, 0:1]), "s")
        bank[b]["r"] = tok_tc
        b2 = 1
        p.wait("T", tok_tc)
        p.wait("T", bank[b2]["r"])
        tok_m2 = p.op("T", lambda E, _b=b2, _w=wi: E.matmul(
            PSAP[_b][:, 0:1], ONES1F[:], TOT[:, _w:_w + 1], start=True, stop=True), "t")
        bank[b2]["w"] = tok_m2
        p.wait("S", tok_m2)
        cmul = 0.7 / NW1
        tn = p.op("S", lambda E, _b=b2, _w=wi, _c=cmul: E.mul(
            out=THR[:, 2 * _w:2 * _w + 1], in_=PSAP[_b][:, 0:1], mul=-_c), "s")
        tp = p.op("S", lambda E, _b=b2, _w=wi, _c=cmul: E.mul(
            out=THR[:, 2 * _w + 1:2 * _w + 2], in_=PSAP[_b][:, 0:1], mul=_c), "s")
        bank[b2]["r"] = tp
        thr_toks[wi] = (tn, tp)

    SWW = {0: SW0, 1: SW1, 2: SW2}
    for h in range(NHEAD):
        tok_w = dma("G", U3[:, 0:6144], wqkvh_d[h], waits=[w3_free])
        tok_bv = dma("G", BVH, bv_d[:, h * 128:(h + 1) * 128], waits=[bvh_free])

        ev = {"K": [None] * 4, "Q": [None] * 4}
        vev = [None] * NT
        rot = 0
        for name, coff, dstb, bcol in (("K", 128, KB, 16 + h), ("Q", 0, QB, h)):
            for tq in range(4):
                b = rot % 8
                rot += 1
                p.wait("T", bank[b]["r"])
                p.wait("T", tok_w)
                p.wait("T", ln1_ready)
                for dc in range(ND):
                    lhs = U3[:, dc * 384 + coff: dc * 384 + coff + 128]
                    rhs = U1[:, dc * 2048 + tq * 512: dc * 2048 + tq * 512 + 512]
                    tk = p.op("T", lambda E, _l=lhs, _r=rhs, _b=b, _dc=dc: E.matmul(
                        PSAP[_b], _l, _r, start=(_dc == 0), stop=(_dc == ND - 1)),
                        "t" if dc == ND - 1 else None)
                bank[b]["w"] = tk
                p.wait("S", tk)
                p.wait("S", tok_bq)
                dst = dstb[:, tq * 512:(tq + 1) * 512]
                te = p.op("S", lambda E, _d=dst, _b=b, _c=bcol: E.activation(
                    out=_d, in_=PSAP[_b], func=AF.Identity,
                    bias=BQS[:, _c:_c + 1]), "s")
                bank[b]["r"] = te
                ev[name][tq] = te
        for tc in range(NT):
            b = rot % 8
            rot += 1
            p.wait("T", bank[b]["r"])
            if tc == 0:
                p.wait("T", tok_bv)
                p.wait("T", tok_consts)
            for dc in range(ND):
                lhs = U1[:, dc * 2048 + tc * 128: dc * 2048 + tc * 128 + 128]
                rhs = U3[:, dc * 384 + 256: dc * 384 + 384]
                p.op("T", lambda E, _l=lhs, _r=rhs, _b=b, _dc=dc: E.matmul(
                    PSAP[_b][:, 0:128], _l, _r, start=(_dc == 0), stop=False))
            tk = p.op("T", lambda E, _b=b: E.matmul(
                PSAP[_b][:, 0:128], ONES1, BVH, start=False, stop=True), "t")
            bank[b]["w"] = tk
            p.wait("V", tk)
            dst = VB[:, tc * 128:(tc + 1) * 128]
            vev[tc] = p.op("V", lambda E, _d=dst, _b=b: E.tensor_copy(
                _d, PSAP[_b][:, 0:128]), "v")
            bank[b]["r"] = vev[tc]
        w3_free = p.tok("t")
        bvh_free = p.tok("t")

        if h == 0 and "qkv0" in tap_d:
            tk = p.tok("s")
            dma("G", tap_d["qkv0"][:, 0:2048], QB[:], waits=[tk, vev[NT - 1]], sem="do")
            dma("G", tap_d["qkv0"][:, 2048:4096], KB[:], sem="do")
            dma("G", tap_d["qkv0"][:, 4096:6144], VB[:], sem="do")

        p.wait("T", tok_mask)
        p.wait("T", tok_idb)
        for j in range(NQ):
            nk = 4 * j + 4
            npair = nk // 2
            p.wait("T", bank[0]["r"])
            p.wait("T", bank[1]["r"])
            qrhs = QB[:, j * 512:(j + 1) * 512]
            tok_grp = None
            exp_toks = [None] * npair
            pbufs = [None] * npair

            def emit_S(pr):
                wi = pr % 3
                SW = SWW[wi]
                bk0, bk1 = 2 + 2 * wi, 3 + 2 * wi
                kt0, kt1 = 2 * pr, 2 * pr + 1
                p.wait("T", bank[bk0]["r"])
                p.wait("T", bank[bk1]["r"])
                p.wait("T", ev["K"][kt1 // 4])
                p.wait("T", ev["Q"][j])
                tok_s = None
                for half, kt in ((0, kt0), (1, kt1)):
                    m = kt - 4 * j
                    sl = SW[:, half * 512:half * 512 + 512]
                    lhs = KB[:, kt * 128:kt * 128 + 128]
                    last = (half == 1 and m < 0)
                    tok_s = p.op("T", lambda E, _o=sl, _l=lhs, _r=qrhs, _m=m: E.matmul(
                        _o, _l, _r, start=True, stop=(_m < 0)),
                        "t" if last else None)
                    if m >= 0:
                        msk = CBUF[:, 384 - 128 * m:896 - 128 * m]
                        tok_s = p.op("T", lambda E, _o=sl, _msk=msk: E.matmul(
                            _o, IDENTB[:], _msk, start=False, stop=True),
                            "t" if half == 1 else None)
                bank[bk0]["w"] = tok_s
                bank[bk1]["w"] = tok_s
                p.wait("S", tok_s)
                p.wait("S", pslot_free[wi])
                pb = PSLOT[wi]
                tok_e = p.op("S", lambda E, _p=pb, _SW=SW: E.activation(
                    out=_p, in_=_SW[:], func=AF.Exp,
                    bias=CONSTC[:, 1:2], scale=ISQ), "s")
                bank[bk0]["r"] = tok_e
                bank[bk1]["r"] = tok_e
                exp_toks[pr] = tok_e
                pbufs[pr] = pb

            def emit_PV(pr):
                nonlocal tok_grp
                wi = pr % 3
                kt0, kt1 = 2 * pr, 2 * pr + 1
                pb = pbufs[pr]
                p.wait("T", exp_toks[pr])
                p.wait("T", vev[kt1])
                for half, kt in ((0, kt0), (1, kt1)):
                    lhsv = VB[:, kt * 128:kt * 128 + 128]
                    prh = pb[:, half * 512:half * 512 + 512]
                    p.op("T", lambda E, _l=lhsv, _p=prh, _st=(pr == 0 and half == 0),
                         _sp=(pr == npair - 1 and half == 1): E.matmul(
                        PO[:], _l, _p, start=_st, stop=_sp))
                    tok_grp = p.op("T", lambda E, _p=prh, _st=(pr == 0 and half == 0),
                                   _sp=(pr == npair - 1 and half == 1): E.matmul(
                        PDEN[0:1, :], ONESCOLB[:], _p, start=_st, stop=_sp),
                        "t")
                pslot_free[wi] = tok_grp

            for pr in range(npair):
                emit_S(pr)
                if pr >= 2:
                    emit_PV(pr - 2)
            emit_PV(npair - 2) if npair >= 2 else None
            emit_PV(npair - 1)
            bank[0]["w"] = tok_grp
            bank[1]["w"] = tok_grp
            dst = U2[:, h * 2048 + j * 512: h * 2048 + (j + 1) * 512]
            p.wait("V", tok_grp)
            tok_oc = p.op("V", lambda E, _d=dst: E.tensor_copy(_d, PO[:]), "v")
            bank[0]["r"] = tok_oc
            invf_j = LNTM[0:1, j * 512:(j + 1) * 512]
            tok_rc = p.op("V", lambda E, _i=invf_j: E.reciprocal(_i, PDEN[0:1, :]), "v")
            bank[1]["r"] = tok_rc
            deferred.append((dst, invf_j, tok_rc))

        for dst, invf_j, tok_rc in deferred:
            rb = 6
            p.wait("T", tok_rc)
            p.wait("T", bank[rb]["r"])
            tok_rep = p.op("T", lambda E, _b=rb, _i=invf_j: E.matmul(
                PSAP[_b], ONES1F[:], _i, start=True, stop=True), "t")
            bank[rb]["w"] = tok_rep
            p.wait("S", tok_rep)
            p.wait("S", irep_free[0])
            tok_ir = p.op("S", lambda E, _b=rb, _i=IREP[0]: E.activation(
                out=_i, in_=PSAP[_b], func=AF.Copy), "s")
            bank[rb]["r"] = tok_ir
            p.wait("V", tok_ir)
            tok_on = p.op("V", lambda E, _d=dst, _i=IREP[0]: E.tensor_tensor(
                _d, _d, _i, ALU.mult), "v")
            irep_free[0] = tok_on
        deferred = []

        for m in range(8 * h, 8 * h + 8):
            wstat_tile(m)
        if h == 15:
            emit_threshold(0)
            emit_threshold(1)

    o_ready = p.tok("v")
    if "o" in tap_d:
        dma("G", tap_d["o"], U2[:], waits=[o_ready], sem="do")

    if "thr" in tap_d:
        dma("Y", tap_d["thr"], THR[:], waits=[thr_toks[1][1]], sem="do")

    BOROW = CBUF[0:1, 0:2048]
    tok_bo = dma("G", BOROW, bo_d, waits=[p.tok("t")])
    wo_free = w3_free
    proj_guard = p.tok("t")
    u2r_guard = None
    last_wo_mm = None
    proj_ev = [[None] * 4 for _ in range(NT)]

    qsc_guard = p.tok("t")
    qsc_guard_v = irep_free[0]
    cq = 0
    for ds in range(4):
        src = wo_d[:, ds * 512:(ds + 1) * 512].rearrange("(a p) c -> p a c", p=128)
        dst = U3[:].rearrange("p (a c) -> p a c", c=512)
        tok_wo = dma("G", dst, src, waits=[wo_free])
        for tc in range(NT):
            b = (ds * NT + tc) % 8
            p.wait("T", bank[b]["r"])
            p.wait("T", tok_wo)
            p.wait("T", o_ready)
            if ds == 0 and tc == 0:
                p.wait("T", tok_bo)
            for dc in range(ND):
                lhs = U2[:, dc * 2048 + tc * 128: dc * 2048 + tc * 128 + 128]
                rhs = U3[:, dc * 512:(dc + 1) * 512]
                p.op("T", lambda E, _l=lhs, _r=rhs, _b=b, _dc=dc: E.matmul(
                    PSAP[_b], _l, _r, start=(_dc == 0), stop=False))
            tk = p.op("T", lambda E, _b=b, _ds=ds: E.matmul(
                PSAP[_b], ONES1, BOROW[0:1, _ds * 512:(_ds + 1) * 512],
                start=False, stop=True), "t")
            bank[b]["w"] = tk
            last_wo_mm = tk
            p.wait("S", tk)
            p.wait("S", proj_guard)
            dst2 = U1[:, tc * 2048 + ds * 512: tc * 2048 + (ds + 1) * 512]
            te = p.op("S", lambda E, _d=dst2, _b=b: E.activation(
                out=_d, in_=PSAP[_b], func=AF.Copy), "s")
            bank[b]["r"] = te
            proj_ev[tc][ds] = te
            for _ in range(2):
                if cq < 128:
                    wi, fcb = (0, cq) if cq < 64 else (1, cq - 64)
                    quant_block(wi, fcb); cq += 1

        wo_free = last_wo_mm
    u2r_guard = last_wo_mm
    if "proj" in tap_d:
        dma("G", tap_d["proj"], U1[:], waits=[p.tok("s"), p.tok("v")], sem="do")

    u5_free = [None, None]
    y_toks = []
    ln2st_tok = [None] * NT
    for tc in range(NT):
        slot = tc % 2
        xs = U4[:, slot * 2048:(slot + 1) * 2048]
        tok_x = dma("Y", xs, x_d[tc * 128:(tc + 1) * 128, :], waits=[u4tok[slot]])
        proj_sl = U1[:, tc * 2048:(tc + 1) * 2048]
        tok_st = ln_stats(proj_sl, slot, [tok_x] + proj_ev[tc])
        p.wait("V", tok_st)
        p.wait("V", u5_free[slot])
        lnv = U5[:, slot * 2048:(slot + 1) * 2048]
        p.wait("V", qq_store_prev[0])
        p.wait("V", qq_store_prev[1])
        tok_ln = p.op("V", lambda E, _p=proj_sl, _l=lnv, _s=slot: E.tensor_scalar(
            _l, _p, RSV[:, _s:_s + 1], NMRS[:, _s:_s + 1],
            ALU.mult, ALU.add), "v")
        lnguard[slot] = tok_ln
        p.wait("V", tok_ln)
        p.wait("V", tok_x)
        p.wait("V", u2r_guard)
        ydst = U2[:, tc * 2048:(tc + 1) * 2048]
        ty = p.op("V", lambda E, _y=ydst, _x=xs, _l=lnv: E.tensor_tensor(
            _y, _x, _l, ALU.add), "v")
        u4tok[slot] = ty
        u5_free[slot] = ty
        y_toks.append(ty)
        ln2st_tok[tc] = ln_stats(ydst, slot, [ty],
                                 out=(WSTATP[:, 2 * tc:2 * tc + 1],
                                      WSTATP[:, 2 * tc + 1:2 * tc + 2]))

    y_ready = p.tok("v")
    if "y" in tap_d:
        dma("G", tap_d["y"], U2[:], waits=[y_ready], sem="do")

    B2ROW = CBUF[0:1, 2560:4608]
    tok_b2l = dma("G", B2ROW, b2_d,
                  waits=[last_wo_mm, p.tok("t"), qq_store_prev[0], qq_store_prev[1],
                         qq_store_prev[2], qq_store_prev[3]])
    p.wait("S", tok_b2l)
    tok_b2 = p.op("S", lambda E: E.mul(out=B2ROW, in_=B2ROW, mul=2.0), "s")
    WSL = [U3[:, i * 2048:(i + 1) * 2048] for i in range(4)]
    u3d_free = [wo_free, wo_free, wo_free, wo_free]
    PR2 = [KB, QB, VB, CBUF[:, 0:2048]]
    pr2_free = [qq_store_prev[2], qq_store_prev[3],
                qq_store_prev[3], qq_store_prev[3]]
    u5w_free = p.tok("v")
    lntm_free = p.tok("t")
    g_guard = None
    u1d_guard = p.tok("v")
    u1d_guard_s = p.tok("s")

    def emit_ln2prep(w):
        last = None
        for tsub in range(4):
            tc = 4 * w + tsub
            ysl = U2[:, tc * 2048:(tc + 1) * 2048]
            stage = U4[:, 0:2048]
            p.wait("S", ln2st_tok[tc])
            p.wait("S", u4tok[0])
            tok_ap = p.op("S", lambda E, _y=ysl, _tc=tc, _o=stage: E.activation(
                out=_o, in_=_y, func=AF.Identity,
                bias=WSTATP[:, 2 * _tc + 1:2 * _tc + 2],
                scale=WSTATP[:, 2 * _tc:2 * _tc + 1]), "s")
            p.wait("T", tok_ap)
            for qtr in range(4):
                hb = 6 + qtr % 2
                p.wait("T", bank[hb]["r"])
                for sl in range(4):
                    dc = qtr * 4 + sl
                    tok_tr = p.op("T", lambda E, _hb=hb, _sl=sl, _dc=dc, _i=stage: E.transpose(
                        PSAP[_hb][:, _sl * 128:_sl * 128 + 128],
                        _i[:, _dc * 128:_dc * 128 + 128], IDENTF[:]),
                        "t" if sl == 3 else None)
                bank[hb]["w"] = tok_tr
                p.wait("V", tok_tr)
                for sl in range(4):
                    dc = qtr * 4 + sl
                    p.wait("V", u5w_free)
                    dst = U5[:, dc * 512 + tsub * 128: dc * 512 + tsub * 128 + 128]
                    tev = p.op("V", lambda E, _d=dst, _hb=hb, _sl=sl: E.tensor_copy(
                        _d, PSAP[_hb][:, _sl * 128:_sl * 128 + 128]), "v")
                    if sl == 3:
                        bank[hb]["r"] = tev
                    last = tev
            u4tok[0] = tok_tr
        return last

    p.wait("V", qq_store_prev[0])
    p.wait("V", qq_store_prev[1])
    ln2_ready_arr = [None] * 4
    ln2_ready_arr[0] = emit_ln2prep(0)
    for w in range(4):
        ln2_ready = ln2_ready_arr[w]

        f1ev = [None] * NF
        for fc in range(NF):
            slot = fc % 4
            tok_wl = dma("G", WSL[slot], w1q8[fc],
                         waits=[u3d_free[slot], w1_store_tok[fc]])
            b = fc % 2
            p.wait("T", bank[b]["r"])
            p.wait("T", tok_wl)
            p.wait("T", ln2_ready)
            for dc in range(ND):
                lhs = WSL[slot][:, dc * 128:dc * 128 + 128]
                rhs = U5[:, dc * 512:(dc + 1) * 512]
                tk = p.op("T", lambda E, _l=lhs, _r=rhs, _b=b, _dc=dc: E.matmul(
                    PSAP[_b], _l, _r, start=(_dc == 0), stop=(_dc == ND - 1)),
                    "t" if dc == ND - 1 else None)
            bank[b]["w"] = tk
            u3d_free[slot] = tk
            p.wait("S", tk)
            if w == 0:
                p.wait("S", u1d_guard_s)
                p.wait("S", u1d_guard)
                if fc == 0:
                    p.wait("S", tok_b1)
            p.wait("S", g_guard)
            dst = U1[:, fc * 512:(fc + 1) * 512]
            te = p.op("S", lambda E, _d=dst, _b=b, _fc=fc: E.activation(
                out=_d, in_=PSAP[_b], func=AF.Gelu,
                bias=B1S[:, _fc:_fc + 1], scale=0.5), "s")
            bank[b]["r"] = te
            f1ev[fc] = te
        if w == 0 and "g0" in tap_d:
            dma("G", tap_d["g0"], U1[:], waits=[p.tok("s")], sem="do")

        proj2_ev = [[None] * 4 for _ in range(4)]
        for q in range(4):
            last_grp = None
            for fcg in range(16):
                slot = fcg % 4
                src = w2q8[q, fcg * 4:fcg * 4 + 4].rearrange("a p c -> p a c")
                dst = WSL[slot].rearrange("p (a c) -> p a c", c=512)
                tok_wl = dma("G", dst, src,
                             waits=[u3d_free[slot], w2_store_tok[q][fcg * 4 + 3]])
                p.wait("T", tok_wl)
                for fc in range(fcg * 4, fcg * 4 + 4):
                    p.wait("T", f1ev[fc])
                    for tsub in range(4):
                        b = 2 + tsub
                        if fc == 0:
                            p.wait("T", bank[b]["r"])
                        last = (fc == fcg * 4 + 3 and tsub == 3)
                        lhs = U1[:, fc * 512 + tsub * 128: fc * 512 + tsub * 128 + 128]
                        rhs = WSL[slot][:, (fc % 4) * 512:(fc % 4) * 512 + 512]
                        last_grp = p.op("T", lambda E, _l=lhs, _r=rhs, _b=b, _fc=fc: E.matmul(
                            PSAP[_b], _l, _r, start=(_fc == 0), stop=False),
                            "t" if last else None)
                u3d_free[slot] = last_grp
            for tsub in range(4):
                b = 2 + tsub
                if q == 0 and tsub == 0:
                    p.wait("T", tok_b2)
                tk = p.op("T", lambda E, _b=b, _q=q: E.matmul(
                    PSAP[_b], ONES1, B2ROW[0:1, _q * 512:(_q + 1) * 512],
                    start=False, stop=True), "t")
                bank[b]["w"] = tk
                p.wait("S", tk)
                p.wait("S", pr2_free[tsub])
                dst = PR2[tsub][:, q * 512:(q + 1) * 512]
                te = p.op("S", lambda E, _d=dst, _b=b: E.activation(
                    out=_d, in_=PSAP[_b], func=AF.Copy, scale=0.5), "s")
                bank[b]["r"] = te
                proj2_ev[tsub][q] = te

        g_guard = p.tok("t")

        u5w_free = f1ev[-1]
        if w < 3:
            ln2_ready_arr[w + 1] = emit_ln2prep(w + 1)

        out_toks = []
        for tsub in range(4):
            tc = 4 * w + tsub
            slot = tsub % 2
            p2 = PR2[tsub][:, 0:2048]
            tok_st = ln_stats(p2, slot, proj2_ev[tsub])
            p.wait("V", tok_st)
            p.wait("V", lntm_free)
            tok_ln = p.op("V", lambda E, _p=p2, _s=slot: E.tensor_scalar(
                LNTM[:], _p, RSV[:, _s:_s + 1], NMRS[:, _s:_s + 1],
                ALU.mult, ALU.add), "v")
            lnguard[slot] = tok_ln
            pr2_free[tsub] = tok_ln
            xs = U4[:, 2048:4096]
            p.wait("V", tok_ln)
            p.wait("V", u4tok[1])
            ysl = U2[:, tc * 2048:(tc + 1) * 2048]
            tv = p.op("V", lambda E, _o=xs, _y=ysl, _l=LNTM: E.tensor_tensor(
                _o, _y, _l[:], ALU.add), "v")
            lntm_free = tv
            td = dma("Y", out_d[tc * 128:(tc + 1) * 128, :], xs, waits=[tv], sem="do")
            u4tok[1] = td
            out_toks.append(td)

    p.wait("Y", p.tok("do"))

    with nc.Block() as block:
        @block.tensor
        def _(E):
            for fn in p.fns["T"]:
                fn(E)

        @block.vector
        def _(E):
            for fn in p.fns["V"]:
                fn(E)

        @block.scalar
        def _(E):
            for fn in p.fns["S"]:
                fn(E)

        @block.sync
        def _(E):
            for fn in p.fns["Y"]:
                fn(E)

        @block.gpsimd
        def _(E):
            for fn in p.fns["G"]:
                fn(E)

    es.close()
    return nc


def make_inputs(inputs, core):
    x = np.asarray(inputs["L0_attn_x"], dtype=np.float32)
    wqkv = np.ascontiguousarray(np.asarray(inputs["L0_attn_Wqkv"], dtype=np.float32))
    bqkv = np.asarray(inputs["L0_attn_bqkv"], dtype=np.float32)
    wo = np.ascontiguousarray(np.asarray(inputs["L0_attn_Wo"], dtype=np.float32))
    bo = np.asarray(inputs["L0_attn_bo"], dtype=np.float32)
    w1 = np.asarray(inputs["L0_ffn_w1"], dtype=np.float32)
    b1 = np.asarray(inputs["L0_ffn_b1"], dtype=np.float32)
    w2 = np.asarray(inputs["L0_ffn_w2"], dtype=np.float32)
    b2 = np.asarray(inputs["L0_ffn_b2"], dtype=np.float32)
    p = np.arange(128)[:, None]
    j = np.arange(896)[None, :]
    maskm = np.where(j >= p + 384, 0.0, -30000.0).astype(np.float32)
    W4 = wqkv.reshape(16, 128, 3, 16, 128)
    wqkvh = np.ascontiguousarray(W4.transpose(3, 1, 0, 2, 4).reshape(16, 128, 6144))
    common = {
        "wqkvh": wqkvh,
        "wo": wo,
        "w1qin": np.ascontiguousarray(w1.reshape(64, 128, 16, 128).transpose(0, 3, 2, 1).reshape(64, 128, 2048)),
        "w2t": np.ascontiguousarray(w2.T),
        "bq": np.ascontiguousarray(bqkv.reshape(48, 128)),
        "bvrow": np.ascontiguousarray(bqkv[4096:].reshape(1, 2048)),
        "borow": np.ascontiguousarray(bo.reshape(1, 2048)),
        "b1": np.ascontiguousarray(b1.reshape(64, 128)),
        "b2row": np.ascontiguousarray(b2.reshape(1, 2048)),
        "maskm": maskm,
        "ident": np.eye(128, dtype=np.float32),
    }
    return {"x": np.ascontiguousarray(x[core]), **common}


def kernel(**inputs):
    nc = build()
    in_maps = [make_inputs(inputs, c) for c in range(8)]
    res = run_bass_kernel_spmd(nc, in_maps, core_ids=list(range(8)))
    return np.stack([np.asarray(res.results[c]["out"]) for c in range(8)], axis=0)
